# version 18
# baseline (speedup 1.0000x reference)
import numpy as np
import ml_dtypes
from contextlib import ExitStack

import concourse.bass as bass
import concourse.mybir as mybir
from concourse.bass_utils import run_bass_kernel_spmd

F32 = mybir.dt.float32
BF16 = mybir.dt.bfloat16
I8 = mybir.dt.int8
AF = mybir.ActivationFunctionType
ALU = mybir.AluOpType

H = 1024
T = 2048
C = 256
NCH = T // C
CT = C // 128
HT = H // 128
FF = 3072
FS = 1024
NS = FF // FS
FT = FS // 128
TC = 512
GAMMA = 0.96875
SCL = 1.0 / 127.0
EPS = 1e-5

ENGINES = ("tensor", "vector", "scalar", "sync", "gpsimd")


class Prog:

    def __init__(self, nc):
        self.nc = nc
        self.streams = {e: [] for e in ENGINES}
        self.sems = {}
        self.counts = {}
        self.waited = {e: {} for e in ENGINES}

    def add_sem(self, name, handle):
        self.sems[name] = handle
        self.counts[name] = 0

    def emit(self, eng, fn, waits=(), inc=None, inc_n=1):
        real = []
        for tok in waits:
            if tok is None:
                continue
            s, v = tok
            if v <= self.waited[eng].get(s, 0):
                continue
            self.waited[eng][s] = v
            real.append((s, v))
        tok = None
        if inc is not None:
            self.counts[inc] += inc_n
            tok = (inc, self.counts[inc])
        self.streams[eng].append((real, fn, inc, inc_n))
        return tok

    def replay(self, block):
        def mk(eng):
            ops = self.streams[eng]

            def run(e):
                for waits, fn, inc, inc_n in ops:
                    for s, v in waits:
                        e.wait_ge(self.sems[s], v)
                    inst = fn(e)
                    if inc is not None:
                        inst.then_inc(self.sems[inc], inc_n)
            return run

        for eng in ENGINES:
            if self.streams[eng]:
                getattr(block, eng)(mk(eng))
        self.streams = {e: [] for e in ENGINES}


class PsumPool:
    def __init__(self, banks):
        self.banks = banks
        self.tok = [None] * len(banks)
        self.i = 0

    def acquire(self):
        b = self.i % len(self.banks)
        self.i += 1
        return b, self.banks[b], self.tok[b]

    def release(self, b, tok):
        self.tok[b] = tok


def build_nc():
    nc = bass.Bass("TRN2", target_bir_lowering=False, dynamic_dma_scratch_size=16384)
    P = Prog(nc)

    xT_d = nc.dram_tensor("xT", [H, T], F32, kind="ExternalInput")
    WqT_d = nc.dram_tensor("WqT", [H, H], F32, kind="ExternalInput")
    WkT_d = nc.dram_tensor("WkT", [H, H], F32, kind="ExternalInput")
    WvT_d = nc.dram_tensor("WvT", [H, H], F32, kind="ExternalInput")
    bq_col_d = nc.dram_tensor("bq_col", [128, HT], F32, kind="ExternalInput")
    bk_col_d = nc.dram_tensor("bk_col", [128, HT], F32, kind="ExternalInput")
    maskT_d = nc.dram_tensor("maskT", [128, CT * C], BF16, kind="ExternalInput")
    gp_d = nc.dram_tensor("gp_bc", [128, T], BF16, kind="ExternalInput")
    wown_d = nc.dram_tensor("wown_col", [128, T // 128], F32, kind="ExternalInput")
    wcore_d = nc.dram_tensor("wcore_col", [128, 1], F32, kind="ExternalInput")
    ident_d = nc.dram_tensor("ident", [128, 128], BF16, kind="ExternalInput")
    WgT_d = nc.dram_tensor("WgT", [H, FF], I8, kind="ExternalInput")
    WuT_d = nc.dram_tensor("WuT", [H, FF], I8, kind="ExternalInput")
    WdT_d = nc.dram_tensor("WdT", [FF, H], I8, kind="ExternalInput")
    out_d = nc.dram_tensor("out", [H, T], F32, kind="ExternalOutput")
    cc_in_d = nc.dram_tensor("cc_in", [128, HT * H], BF16, kind="Internal")
    cc_out_d = nc.dram_tensor("cc_out", [256, HT * H], BF16, kind="Internal")
    cc2_in_d = nc.dram_tensor("cc2_in", [128, 2 * CT * H], BF16, kind="Internal")
    cc2_out_d = nc.dram_tensor("cc2_out", [256, 2 * CT * H], BF16, kind="Internal")
    cc0_in_d = nc.dram_tensor("cc0_in", [1, 16], F32, kind="Internal")
    cc0_out_d = nc.dram_tensor("cc0_out", [8, 16], F32, kind="Internal")

    with ExitStack() as ctx:
        E = ctx.enter_context
        for s in ("sT", "sV", "sS", "sG", "sY", "dX", "dA", "dWkv", "dWq", "dCC",
                  "dCi", "dCk", "dCv", "dC2", "dC2b", "dF0", "dF1", "dWd", "dO"):
            P.add_sem(s, E(nc.semaphore(s)))

        xT = E(nc.sbuf_tensor("xT_sb", [128, HT * T], F32))
        ones_col = E(nc.sbuf_tensor("ones_col", [128, 1], BF16))
        ones_row = E(nc.sbuf_tensor("ones_row", [1, 128], BF16))
        eps_sb = E(nc.sbuf_tensor("eps_sb", [1, 1], F32))
        rows_own = {}
        rows_own["rstd_bf"] = E(nc.sbuf_tensor("ro_rstd_bf", [1, T], BF16))
        rows_own["mr_bf"] = E(nc.sbuf_tensor("ro_mr_bf", [1, T], BF16))
        rbc = E(nc.sbuf_tensor("rbc", [128, C], BF16))
        mrbc = E(nc.sbuf_tensor("mrbc", [128, C], BF16))
        psum = [E(nc.psum_tensor(f"ps{i}", [128, 512], F32)) for i in range(7)]
        pool = PsumPool(psum)
        pst = E(nc.psum_tensor("pst", [128, 1024], BF16))
        cast_t = sqr_t = rows_pro = None

        tok_consts = None
        stats_guard = {}
        bc_guard = {}

        def emit_stats_chunk(src_of, o0, w, rows, waits):
            bs, bank_s, busy_s = pool.acquire()
            bqi, bank_q, busy_q = pool.acquire()
            tok_sum = tok_sq = None
            for i in range(HT):
                buf = cast_t[i % 2]
                prev = stats_guard.get(i % 2)
                src = src_of(i)
                ct = P.emit("scalar",
                            lambda e, o=buf[:, :w], s=src: e.copy(o, s),
                            waits=list(waits) + [prev], inc="sS")
                tok_sum = P.emit(
                    "tensor",
                    lambda e, o=bank_s[:1, :w], r=buf[:, :w], s1=(i == 0), s2=(i == HT - 1):
                    e.matmul(o, ones_col[:, :1], r, start=s1, stop=s2),
                    waits=([ct, busy_s, tok_consts] if i == 0 else [ct]),
                    inc="sT")
                st = P.emit("gpsimd",
                            lambda e, o=buf[:, :w]: e.tensor_mul(o, o, o),
                            waits=[tok_sum], inc="sG")
                tok_sq = P.emit(
                    "tensor",
                    lambda e, o=bank_q[:1, :w], r=buf[:, :w], s1=(i == 0), s2=(i == HT - 1):
                    e.matmul(o, ones_col[:, :1], r, start=s1, stop=s2),
                    waits=([st, busy_q] if i == 0 else [st]),
                    inc="sT")
                stats_guard[i % 2] = tok_sq
            t1 = P.emit("vector",
                        lambda e, o=rows["m"][:1, o0:o0 + w], s=bank_s[:1, :w]:
                        e.tensor_scalar_mul(o, s, 1.0 / H),
                        waits=[tok_sum, tok_sq], inc="sV")
            t2 = P.emit("vector",
                        lambda e, o=rows["var"][:1, o0:o0 + w], a=rows["m"][:1, o0:o0 + w]:
                        e.tensor_mul(o, a, a), waits=[t1], inc="sV")
            t3 = P.emit("vector",
                        lambda e, o=rows["var"][:1, o0:o0 + w], s=bank_q[:1, :w],
                        b=rows["var"][:1, o0:o0 + w]:
                        e.scalar_tensor_tensor(o, s, 1.0 / H, b, ALU.mult, ALU.subtract),
                        waits=[t2], inc="sV")
            pool.release(bs, t1)
            pool.release(bqi, t3)
            return t3

        def emit_rows_finalize(rows, n_tok, rows_tok, scratch=None):
            mid = scratch if scratch is not None else rows["var"][:1, :n_tok]
            a1 = P.emit("scalar",
                        lambda e, o=mid, i_=rows["var"][:1, :n_tok]:
                        e.activation(o, i_, AF.Ln, bias=eps_sb[:1, :1], scale=1.0),
                        waits=[rows_tok, tok_consts], inc="sS")
            a2 = P.emit("scalar",
                        lambda e, o=rows["var"][:1, :n_tok], i_=mid:
                        e.activation(o, i_, AF.Exp, scale=-0.5), waits=[a1], inc="sS")
            b1 = P.emit("vector",
                        lambda e, o=rows["rstd_bf"][:1, :n_tok], i_=rows["var"][:1, :n_tok]:
                        e.tensor_copy(o, i_), waits=[a2], inc="sV")
            b2 = P.emit("vector",
                        lambda e, o=rows["m"][:1, :n_tok], a=rows["m"][:1, :n_tok],
                        b=rows["var"][:1, :n_tok]: e.tensor_mul(o, a, b),
                        waits=[a2, b1], inc="sV")
            b3 = P.emit("vector",
                        lambda e, o=rows["mr_bf"][:1, :n_tok], i_=rows["m"][:1, :n_tok]:
                        e.tensor_copy(o, i_), waits=[b2], inc="sV")
            return b3

        def emit_ln_apply(src_of, dst_of, cw, rows, roff, rows_tok, extra_waits=(),
                          bc=None):
            rbc_, mrbc_ = bc if bc is not None else (rbc, mrbc)
            br, bank_r, busy_r = pool.acquire()
            tr = P.emit("tensor",
                        lambda e, o=bank_r[:128, :cw], r=rows["rstd_bf"][:1, roff:roff + cw]:
                        e.matmul(o, ones_row[:1, :128], r, start=True, stop=True),
                        waits=[rows_tok, busy_r, tok_consts, tok_dA], inc="sT")
            bm, bank_m, busy_m = pool.acquire()
            tm = P.emit("tensor",
                        lambda e, o=bank_m[:128, :cw], r=rows["mr_bf"][:1, roff:roff + cw]:
                        e.matmul(o, ones_row[:1, :128], r, start=True, stop=True),
                        waits=[busy_m], inc="sT")
            e1 = P.emit("vector", lambda e, o=rbc_[:, :cw], s=bank_r[:128, :cw]:
                        e.tensor_copy(o, s),
                        waits=[tr, bc_guard.get(id(rbc_))], inc="sV")
            e2 = P.emit("vector", lambda e, o=mrbc_[:, :cw], s=bank_m[:128, :cw]:
                        e.tensor_copy(o, s),
                        waits=[tm, bc_guard.get(id(rbc_))], inc="sV")
            pool.release(br, e1)
            pool.release(bm, e2)
            tok = None
            for i in range(HT):
                tma = P.emit("vector",
                             lambda e, o=dst_of(i), s=src_of(i), m=rbc_[:, :cw]:
                             e.tensor_mul(o, s, m),
                             waits=list(extra_waits) + [e1, e2], inc="sV")
                tok = P.emit("vector",
                             lambda e, o=dst_of(i), m=mrbc_[:, :cw]:
                             e.tensor_sub(o, o, m), waits=[tma], inc="sV")
            bc_guard[id(rbc_)] = tok
            return tok

        with ExitStack() as mctx:
            M = mctx.enter_context
            cast_t = [M(nc.sbuf_tensor(f"castt{i}", [128, TC], BF16)) for i in range(2)]
            for n in ("m", "var"):
                rows_own[n] = M(nc.sbuf_tensor(f"ro_{n}", [1, T], BF16))
            qsc_all = M(nc.sbuf_tensor("qsc_all", [128, HT * T], BF16))
            S_sb = M(nc.sbuf_tensor("S_sb", [128, HT * H], BF16))
            wcore = M(nc.sbuf_tensor("wcore_sb", [128, 1], F32))
            gp_bc = M(nc.sbuf_tensor("gp_sb", [128, T], BF16))
            actx = mctx.enter_context(ExitStack())
            A = actx.enter_context
            WqT = A(nc.sbuf_tensor("WqT_sb", [128, HT * H], BF16))
            WkT = A(nc.sbuf_tensor("WkT_sb", [128, HT * H], BF16))
            WvT = A(nc.sbuf_tensor("WvT_sb", [128, HT * H], BF16))
            bq_col = A(nc.sbuf_tensor("bq_sb", [128, HT], F32))
            bk_col = A(nc.sbuf_tensor("bkc_sb", [128, HT], F32))
            maskT = A(nc.sbuf_tensor("maskT_sb", [128, CT * C], BF16))
            wown = A(nc.sbuf_tensor("wown_sb", [128, T // 128], F32))
            ident = A(nc.sbuf_tensor("ident_sb", [128, 128], BF16))
            lnT = [A(nc.sbuf_tensor(f"lnT{i}", [128, HT * C], BF16)) for i in range(2)]
            k_sb = A(nc.sbuf_tensor("k_sb", [128, HT * C], BF16))
            ktm = A(nc.sbuf_tensor("ktm_sb", [128, CT * H], BF16))
            v_sb = A(nc.sbuf_tensor("v_sb", [128, CT * H], BF16))
            scT = A(nc.sbuf_tensor("scT_sb", [128, CT * C], BF16))

            for i in range(HT):
                P.emit("gpsimd", lambda e, i=i: e.dma_start(
                    xT[:, i * T:(i + 1) * T], xT_d.ap()[i * 128:(i + 1) * 128, :]),
                    inc="dX", inc_n=16)
            for dst, src in ((maskT, maskT_d), (gp_bc, gp_d), (wown, wown_d),
                             (wcore, wcore_d), (bq_col, bq_col_d), (bk_col, bk_col_d),
                             (ident, ident_d)):
                P.emit("sync", lambda e, o=dst, s=src: e.dma_start(o[:], s.ap()),
                       inc="dA", inc_n=16)
            tok_dA = ("dA", 16 * 7)
            tok_dX = ("dX", 16 * 8)

            for i in range(HT):
                P.emit("gpsimd", lambda e, i=i: e.dma_start(
                    WqT[:, i * H:(i + 1) * H], WqT_d.ap()[i * 128:(i + 1) * 128, :]),
                    inc="dWq", inc_n=16)
            tok_dWq = ("dWq", 16 * 8)
            for W_sb, W_d in ((WkT, WkT_d), (WvT, WvT_d)):
                for i in range(HT):
                    P.emit("gpsimd", lambda e, o=W_sb, d=W_d, i=i: e.dma_start(
                        o[:, i * H:(i + 1) * H], d.ap()[i * 128:(i + 1) * 128, :]),
                        inc="dWkv", inc_n=16)
            tok_dWkv = ("dWkv", 16 * 16)

            P.emit("vector", lambda e: e.memset(ones_col[:], 1.0))
            P.emit("vector", lambda e: e.memset(ones_row[:1, :], 1.0))
            P.emit("vector", lambda e: e.memset(eps_sb[:1, :1], EPS))
            tok_consts = P.emit("vector", lambda e: e.memset(S_sb[:], 0.0), inc="sV")

            S_tok = [tok_consts]
            cross_done = []
            pst_busy = [None]

            def proj_fm(W_sb, out_of, b_col, lnb, ln_tok, w_tok):
                tok = None
                for mo in range(HT):
                    bb, bank, busy = pool.acquire()
                    for i in range(HT):
                        w = [ln_tok, w_tok, busy, tok_dX, tok_dA] if i == 0 else ()
                        P.emit("tensor",
                               lambda e, o=bank[:128, :C],
                               a=W_sb[:, i * H + mo * 128: i * H + mo * 128 + 128],
                               r=lnb[:, i * C:(i + 1) * C], s1=(i == 0), s2=(i == HT - 1):
                               e.matmul(o, a, r, start=s1, stop=s2),
                               waits=w, inc=("sT" if i == HT - 1 else None))
                    gtok = ("sT", P.counts["sT"])
                    tok = P.emit("scalar",
                                 lambda e, o=out_of(mo),
                                 s=bank[:128, :C], b=b_col[:, mo:mo + 1]:
                                 e.activation(o, s, AF.Identity, bias=b, scale=1.0),
                                 waits=[gtok, tok_dA], inc="sS")
                    pool.release(bb, tok)
                return tok

            def proj_tm(W_sb, out_sb, scale_of, bias_row, lnb, ln_tok, w_tok):
                tok = None
                for mt in range(CT):
                    for no in range(H // 512):
                        bb, bank, busy = pool.acquire()
                        npairs = HT + (1 if bias_row is not None else 0)
                        for i in range(HT):
                            w = [ln_tok, w_tok, busy, tok_dX, tok_dA] if i == 0 else ()
                            P.emit("tensor",
                                   lambda e, o=bank[:128, :512],
                                   a=lnb[:, i * C + mt * 128: i * C + mt * 128 + 128],
                                   r=W_sb[:, i * H + no * 512: i * H + (no + 1) * 512],
                                   s1=(i == 0), s2=(i == npairs - 1):
                                   e.matmul(o, a, r, start=s1, stop=s2),
                                   waits=w, inc=("sT" if i == npairs - 1 else None))
                        gtok = ("sT", P.counts["sT"])
                        dst = out_sb[:, mt * H + no * 512: mt * H + (no + 1) * 512]
                        if scale_of is not None:
                            sc = scale_of(mt)
                            tok = P.emit("vector",
                                         lambda e, o=dst, s=bank[:128, :512], c=sc:
                                         e.tensor_scalar(o, s, c, None, ALU.mult),
                                         waits=[gtok, tok_dA], inc="sV")
                        else:
                            tok = P.emit("scalar",
                                         lambda e, o=dst, s=bank[:128, :512]: e.copy(o, s),
                                         waits=[gtok], inc="sS")
                        pool.release(bb, tok)
                return tok

            def emit_state_update(v_tok, ktm_tok):
                nonlocal S_tok, cross_done
                new_tok = []
                for nh in (1, 0):
                    for mo in range(HT):
                        bb, bank, busy = pool.acquire()
                        for j in range(CT):
                            w = [v_tok, ktm_tok, busy] if j == 0 else ()
                            P.emit("tensor",
                                   lambda e, o=bank[:128, :512],
                                   a=ktm[:, j * H + mo * 128: j * H + mo * 128 + 128],
                                   r=v_sb[:, j * H + nh * 512: j * H + (nh + 1) * 512],
                                   s1=(j == 0), s2=(j == CT - 1):
                                   e.matmul(o, a, r, start=s1, stop=s2),
                                   waits=w, inc=("sT" if j == CT - 1 else None))
                        gtok = ("sT", P.counts["sT"])
                        sl = S_sb[:, mo * H + nh * 512: mo * H + (nh + 1) * 512]
                        cd = (cross_done[4 * nh:4 * nh + 4]
                              if len(cross_done) == HT else cross_done)
                        ut = P.emit("vector",
                                    lambda e, o=sl, s=bank[:128, :512]: e.tensor_add(o, o, s),
                                    waits=[gtok] + cd + S_tok, inc="sV")
                        pool.release(bb, ut)
                        new_tok.append(ut)
                S_tok = new_tok
                cross_done = []

            def emit_chunk(c_idx, lnb, ln_tok, mid_hook=None, do_update=True):
                nonlocal cross_done
                res_toks = []
                qof = lambda mo, c=c_idx: qsc_all[:, mo * T + c * C: mo * T + (c + 1) * C]
                q_tok = proj_fm(WqT, qof, bq_col, lnb, ln_tok, tok_dWq)
                k_tok = proj_fm(WkT, lambda mo: k_sb[:, mo * C:(mo + 1) * C],
                                bk_col, lnb, ln_tok, tok_dWkv)
                ktm_tok = None
                for mt in range(CT):
                    for i in range(HT):
                        w = [k_tok, pst_busy[0], tok_dA] if i == 0 else ()
                        P.emit("tensor",
                               lambda e, o=pst[:, i * 128:(i + 1) * 128],
                               s=k_sb[:, i * C + mt * 128: i * C + (mt + 1) * 128]:
                               e.transpose(o, s, ident[:]),
                               waits=w, inc=("sT" if i == HT - 1 else None))
                    gtok = ("sT", P.counts["sT"])
                    ktm_tok = P.emit("vector",
                                     lambda e, o=ktm[:, mt * H:(mt + 1) * H],
                                     s=pst[:, :], c=wown[:, c_idx * CT + mt: c_idx * CT + mt + 1]:
                                     e.tensor_scalar(o, s, c, None, ALU.mult),
                                     waits=[gtok], inc="sV")
                    pst_busy[0] = ktm_tok
                v_tok = proj_tm(WvT, v_sb, None, None, lnb, ln_tok, tok_dWkv)

                sc_mm, sc_ev = [], []
                for mj in range(CT):
                    bb, bank, busy = pool.acquire()
                    for i in range(HT):
                        w = [q_tok, k_tok, busy] if i == 0 else ()
                        P.emit("tensor",
                               lambda e, o=bank[:128, :C],
                               a=k_sb[:, i * C + mj * 128: i * C + mj * 128 + 128],
                               r=qof(i),
                               s1=(i == 0), s2=(i == HT - 1):
                               e.matmul(o, a, r, start=s1, stop=s2),
                               waits=w, inc=("sT" if i == HT - 1 else None))
                    gtok = ("sT", P.counts["sT"])
                    sc_mm.append(gtok)
                    ev = P.emit("vector",
                                lambda e, o=scT[:, mj * C:(mj + 1) * C],
                                s=bank[:128, :C], m=maskT[:, mj * C:(mj + 1) * C]:
                                e.tensor_mul(o, s, m),
                                waits=[gtok, tok_dA], inc="sV")
                    sc_ev.append(ev)
                    pool.release(bb, ev)
                qsc_tok = None
                for mo in range(HT):
                    qsc_tok = P.emit("vector",
                                     lambda e, o=qof(mo),
                                     g=gp_bc[:, c_idx * C:(c_idx + 1) * C]:
                                     e.tensor_mul(o, o, g),
                                     waits=sc_mm + [q_tok, tok_dA], inc="sV")
                if mid_hook is not None:
                    mid_hook()
                for mh in range(HT):
                    bb, bank, busy = pool.acquire()
                    pairs = [(v_sb[:, j * H + mh * 128: j * H + mh * 128 + 128],
                              scT[:, j * C:(j + 1) * C]) for j in range(CT)]
                    pairs += [(S_sb[:, i * H + mh * 128: i * H + mh * 128 + 128],
                               qof(i)) for i in range(HT)]
                    for idx, (a_, r_) in enumerate(pairs):
                        w = sc_ev + [qsc_tok, v_tok, busy] + S_tok if idx == 0 else ()
                        P.emit("tensor",
                               lambda e, o=bank[:128, :C], a=a_, r=r_,
                               s1=(idx == 0), s2=(idx == len(pairs) - 1):
                               e.matmul(o, a, r, start=s1, stop=s2),
                               waits=w, inc=("sT" if idx == len(pairs) - 1 else None))
                    gtok = ("sT", P.counts["sT"])
                    cross_done.append(gtok)
                    sl = xT[:, mh * T + c_idx * C: mh * T + (c_idx + 1) * C]
                    ev = P.emit("vector",
                                lambda e, o=sl, s=bank[:128, :C]: e.tensor_add(o, o, s),
                                waits=[gtok], inc="sV")
                    res_toks.append(ev)
                    pool.release(bb, ev)
                if do_update:
                    emit_state_update(v_tok, ktm_tok)
                return res_toks, qsc_tok, ktm_tok, v_tok

            t3 = None
            for ck in range(T // TC):
                t3 = emit_stats_chunk(
                    lambda i, ck=ck: xT[:, i * T + ck * TC: i * T + (ck + 1) * TC],
                    ck * TC, TC, rows_own, waits=[tok_dX, tok_dA])
            rows1_tok = emit_rows_finalize(rows_own, T, t3)

            def prepare_unit(u):
                return emit_ln_apply(
                    lambda i, c=u: xT[:, i * T + c * C: i * T + (c + 1) * C],
                    lambda i, p=u % 2: lnT[p][:, i * C:(i + 1) * C], C,
                    rows_own, u * C, rows1_tok)

            ln_tok = prepare_unit(0)
            res_by_chunk = {}
            qsc_last = None
            for u in range(NCH):
                nxt = {}

                def mid_hook(u=u, nxt=nxt):
                    if u + 1 < NCH:
                        nxt["tok"] = prepare_unit(u + 1)

                if u == NCH - 1:
                    P.emit("gpsimd", lambda e: e.dma_start(cc_in_d.ap(), S_sb[:]),
                           waits=S_tok, inc="dCC", inc_n=16)
                res, qsc_last, ktm7_tok, v7_tok = emit_chunk(
                    u, lnb=lnT[u % 2], ln_tok=ln_tok, mid_hook=mid_hook,
                    do_update=(u != NCH - 1))
                res_by_chunk[u] = res
                ln_tok = nxt.get("tok")

            P.emit("sync", lambda e: e.dma_start(cc2_in_d.ap()[:, 0:CT * H], ktm[:]),
                   waits=[ktm7_tok], inc="dC2", inc_n=16)
            P.emit("sync", lambda e: e.dma_start(cc2_in_d.ap()[:, CT * H:], v_sb[:]),
                   waits=[v7_tok], inc="dC2b", inc_n=16)
            P.emit("gpsimd", lambda e: e.collective_compute(
                "AllGather", ALU.bypass,
                ins=[cc_in_d.ap().opt()], outs=[cc_out_d.ap().opt()],
                replica_groups=[[0, 1], [2, 3], [4, 5], [6, 7]]),
                waits=[("dCC", 16)], inc="sG")
            tok_cc1 = ("sG", P.counts["sG"])
            P.emit("gpsimd", lambda e: e.collective_compute(
                "AllGather", ALU.bypass,
                ins=[cc2_in_d.ap().opt()], outs=[cc2_out_d.ap().opt()],
                replica_groups=[[0, 1], [2, 3], [4, 5], [6, 7]]),
                waits=[("dC2", 16), ("dC2b", 16)], inc="sG")
            tok_cc2 = ("sG", P.counts["sG"])
            P.emit("gpsimd", lambda e: e.dma_start(S_sb[:], cc_out_d.ap()[0:128, :]),
                   waits=[tok_cc1, cross_done[-1]], inc="dCi", inc_n=16)
            P.emit("gpsimd", lambda e: e.dma_start(ktm[:], cc2_out_d.ap()[0:128, 0:CT * H]),
                   waits=[tok_cc2], inc="dCk", inc_n=16)
            P.emit("gpsimd", lambda e: e.dma_start(v_sb[:], cc2_out_d.ap()[0:128, CT * H:]),
                   inc="dCv", inc_n=16)
            tok_Sscale = P.emit("vector",
                                lambda e: e.tensor_scalar(S_sb[:], S_sb[:],
                                                          wcore[:, :1], None, ALU.mult),
                                waits=[("dCi", 16), tok_dA], inc="sV")
            tok_Sinit = None
            for nh in range(H // 512):
                for mo in range(HT):
                    bb, bank, busy = pool.acquire()
                    for j in range(CT):
                        w = [("dCk", 16), ("dCv", 16), busy] if j == 0 else ()
                        P.emit("tensor",
                               lambda e, o=bank[:128, :512],
                               a=ktm[:, j * H + mo * 128: j * H + mo * 128 + 128],
                               r=v_sb[:, j * H + nh * 512: j * H + (nh + 1) * 512],
                               s1=(j == 0), s2=(j == CT - 1):
                               e.matmul(o, a, r, start=s1, stop=s2),
                               waits=w, inc=("sT" if j == CT - 1 else None))
                    gtok = ("sT", P.counts["sT"])
                    sl = S_sb[:, mo * H + nh * 512: mo * H + (nh + 1) * 512]
                    tok_Sinit = P.emit("vector",
                                       lambda e, o=sl, s=bank[:128, :512]:
                                       e.scalar_tensor_tensor(o, s, wcore[:, :1], o,
                                                              ALU.mult, ALU.add),
                                       waits=[gtok, tok_Sscale], inc="sV")
                    pool.release(bb, tok_Sinit)

            with nc.Block() as blockA1:
                P.replay(blockA1)
            actx.close()
            wg_sb = [E(nc.sbuf_tensor("wg0", [128, HT * FS], BF16, side="right")), None]
            wu_sb = [E(nc.sbuf_tensor("wu0", [128, HT * FS], BF16, side="right")), None]
            wd_sb = E(nc.sbuf_tensor("wd0", [128, FT * H], BF16, side="right"))

            def stripe_dma(s, gate_tok):
                par = s % 2
                P.emit("gpsimd",
                       lambda e, s=s, p=par: e.dma_start(
                           bass.AP(wg_sb[p], 0, [[HT * FS, 128], [FS, HT], [1, FS]]),
                           bass.AP(WgT_d, s * FS, [[FF, 128], [128 * FF, HT], [1, FS]])),
                       waits=[gate_tok], inc=f"dF{par}", inc_n=16)
                P.emit("gpsimd",
                       lambda e, s=s, p=par: e.dma_start(
                           bass.AP(wu_sb[p], 0, [[HT * FS, 128], [FS, HT], [1, FS]]),
                           bass.AP(WuT_d, s * FS, [[FF, 128], [128 * FF, HT], [1, FS]])),
                       inc=f"dF{par}", inc_n=16)
                return (f"dF{par}", P.counts[f"dF{par}"])

            def wd_dma(s, gate_tok):
                P.emit("gpsimd",
                       lambda e, s=s: e.dma_start(
                           bass.AP(wd_sb, 0, [[FT * H, 128], [H, FT], [1, H]]),
                           bass.AP(WdT_d, s * FS * H, [[H, 128], [128 * H, FT], [1, H]])),
                       waits=[gate_tok], inc="dWd", inc_n=16)
                return ("dWd", P.counts["dWd"])

            tok_F = [stripe_dma(0, None), None, None]
            tok_Fd = wd_dma(0, None)

            ln2_stat_tok = None
            for tc in range(T // TC):
                pb_toks = []
                for mh in range(HT):
                    bb, bank, busy = pool.acquire()
                    for i in range(HT):
                        w = [tok_Sinit, qsc_last, busy] if i == 0 else ()
                        P.emit("tensor",
                               lambda e, o=bank[:128, :512],
                               a=S_sb[:, i * H + mh * 128: i * H + mh * 128 + 128],
                               r=qsc_all[:, i * T + tc * TC: i * T + (tc + 1) * TC],
                               s1=(i == 0), s2=(i == HT - 1):
                               e.matmul(o, a, r, start=s1, stop=s2),
                               waits=w, inc=("sT" if i == HT - 1 else None))
                    gtok = ("sT", P.counts["sT"])
                    sl = xT[:, mh * T + tc * TC: mh * T + (tc + 1) * TC]
                    ev = P.emit("vector",
                                lambda e, o=sl, s=bank[:128, :512]: e.tensor_add(o, o, s),
                                waits=[gtok], inc="sV")
                    pb_toks.append(ev)
                    pool.release(bb, ev)
                ln2_stat_tok = emit_stats_chunk(
                    lambda i, tc=tc: xT[:, i * T + tc * TC: i * T + (tc + 1) * TC],
                    tc * TC, TC, rows_own, waits=pb_toks)
            rows2_tok = emit_rows_finalize(
                rows_own, T, ln2_stat_tok,
                scratch=S_sb[0:1, :].bitcast(F32)[:1, :T])

            with nc.Block() as blockA2:
                P.replay(blockA2)

        with ExitStack() as fctx:
            B = fctx.enter_context
            ln2T = B(nc.sbuf_tensor("ln2T_sb", [128, HT * T], BF16))
            wg_sb[1] = B(nc.sbuf_tensor("wg1", [128, HT * FS], BF16))
            wu_sb[1] = B(nc.sbuf_tensor("wu1", [128, HT * FS], BF16))
            gsil = B(nc.sbuf_tensor("gsil_sb", [128, FT * TC], BF16))
            h_sb = B(nc.sbuf_tensor("h_sb", [128, FT * TC], BF16))
            rbc_b = B(nc.sbuf_tensor("rbc_b", [128, TC], BF16))
            mrbc_b = B(nc.sbuf_tensor("mrbc_b", [128, TC], BF16))

            tok_F[1] = stripe_dma(1, None)
            ln2_toks = []
            for ck in range(T // TC):
                ln2_toks.append(emit_ln_apply(
                    lambda i, ck=ck: xT[:, i * T + ck * TC: i * T + ck * TC + TC],
                    lambda i, ck=ck: ln2T[:, i * T + ck * TC: i * T + ck * TC + TC],
                    TC, rows_own, ck * TC, rows2_tok, bc=(rbc_b, mrbc_b)))

            h_prev_tok = None
            for s in range(NS):
                par = s % 2
                wg, wu, wd = wg_sb[par], wu_sb[par], wd_sb
                last_w_consumer = None
                for tc in range(T // TC):
                    gs_toks = []
                    for mf in range(FT):
                        bb, bank, busy = pool.acquire()
                        for i in range(HT):
                            w = [ln2_toks[tc], tok_F[s], busy] if i == 0 else ()
                            P.emit("tensor",
                                   lambda e, o=bank[:128, :TC],
                                   a=wg[:, i * FS + mf * 128: i * FS + mf * 128 + 128],
                                   r=ln2T[:, i * T + tc * TC: i * T + (tc + 1) * TC],
                                   s1=(i == 0), s2=(i == HT - 1):
                                   e.matmul(o, a, r, start=s1, stop=s2),
                                   waits=w, inc=("sT" if i == HT - 1 else None))
                        gtok = ("sT", P.counts["sT"])
                        ev = P.emit("scalar",
                                    lambda e, o=gsil[:, mf * TC:(mf + 1) * TC],
                                    s=bank[:128, :TC]:
                                    e.activation(o, s, AF.Silu, scale=SCL),
                                    waits=[gtok, h_prev_tok], inc="sS")
                        gs_toks.append(ev)
                        pool.release(bb, ev)
                    h_toks = []
                    for mf in range(FT):
                        bb, bank, busy = pool.acquire()
                        for i in range(HT):
                            w = [ln2_toks[tc], tok_F[s], busy] if i == 0 else ()
                            P.emit("tensor",
                                   lambda e, o=bank[:128, :TC],
                                   a=wu[:, i * FS + mf * 128: i * FS + mf * 128 + 128],
                                   r=ln2T[:, i * T + tc * TC: i * T + (tc + 1) * TC],
                                   s1=(i == 0), s2=(i == HT - 1):
                                   e.matmul(o, a, r, start=s1, stop=s2),
                                   waits=w, inc=("sT" if i == HT - 1 else None))
                        gtok = ("sT", P.counts["sT"])
                        ht = P.emit("vector",
                                    lambda e, o=h_sb[:, mf * TC:(mf + 1) * TC],
                                    s=bank[:128, :TC], g=gsil[:, mf * TC:(mf + 1) * TC]:
                                    e.scalar_tensor_tensor(o, s, SCL, g, ALU.mult, ALU.mult),
                                    waits=[gtok, gs_toks[mf]], inc="sV")
                        h_toks.append(ht)
                        pool.release(bb, ht)
                    h_prev_tok = h_toks[-1]
                    for md in range(HT):
                        bb, bank, busy = pool.acquire()
                        for j in range(FT):
                            w = h_toks + [busy, tok_Fd] if j == 0 else ()
                            P.emit("tensor",
                                   lambda e, o=bank[:128, :TC],
                                   a=wd[:, j * H + md * 128: j * H + md * 128 + 128],
                                   r=h_sb[:, j * TC:(j + 1) * TC],
                                   s1=(j == 0), s2=(j == FT - 1):
                                   e.matmul(o, a, r, start=s1, stop=s2),
                                   waits=w, inc=("sT" if j == FT - 1 else None))
                        gtok = ("sT", P.counts["sT"])
                        sl = xT[:, md * T + tc * TC: md * T + (tc + 1) * TC]
                        ev = P.emit("vector",
                                    lambda e, o=sl, s=bank[:128, :TC]:
                                    e.scalar_tensor_tensor(o, s, SCL, o, ALU.mult, ALU.add),
                                    waits=[gtok], inc="sV")
                        pool.release(bb, ev)
                        last_w_consumer = gtok
                if s + 2 < NS:
                    tok_F[s + 2] = stripe_dma(s + 2, last_w_consumer)
                if s + 1 < NS:
                    tok_Fd = wd_dma(s + 1, last_w_consumer)

            with nc.Block() as blockB:
                P.replay(blockB)

        for i in range(HT):
            P.emit("sync", lambda e, i=i: e.dma_start(
                out_d.ap()[i * 128:(i + 1) * 128, :], xT[:, i * T:(i + 1) * T]),
                inc="dO", inc_n=16)
        P.emit("sync", lambda e: e.nop(), waits=[("dO", 16 * HT)])

        with nc.Block() as blockC:
            P.replay(blockC)

    return nc



_NC_CACHE = {}


def _get_nc():
    if "nc" not in _NC_CACHE:
        _NC_CACHE["nc"] = build_nc()
    return _NC_CACHE["nc"]


def _int8_view(w_int32):
    return np.ascontiguousarray(w_int32.view(np.int8)[..., 0::4])


def kernel(L0_attn_x, L0_attn_Wq, L0_attn_bq, L0_attn_Wk, L0_attn_bk, L0_attn_Wv,
           L0_ffn_w_gate, L0_ffn_w_up, L0_ffn_w_down):
    x = np.ascontiguousarray(np.asarray(L0_attn_x, dtype=np.float32))
    Wq = np.asarray(L0_attn_Wq, dtype=np.float32)
    Wk = np.asarray(L0_attn_Wk, dtype=np.float32)
    Wv = np.asarray(L0_attn_Wv, dtype=np.float32)
    bq = np.asarray(L0_attn_bq, dtype=np.float32)
    bk = np.asarray(L0_attn_bk, dtype=np.float32)
    wg8 = _int8_view(np.asarray(L0_ffn_w_gate, dtype=np.int32))
    wu8 = _int8_view(np.asarray(L0_ffn_w_up, dtype=np.int32))
    wd8 = _int8_view(np.asarray(L0_ffn_w_down, dtype=np.int32))

    WqT = np.ascontiguousarray(Wq.T)
    WkT = np.ascontiguousarray(Wk.T)
    WvT = np.ascontiguousarray(Wv.T)
    WgT = np.ascontiguousarray(wg8.T)
    WuT = np.ascontiguousarray(wu8.T)
    WdT = np.ascontiguousarray(wd8.T)
    bq_col = np.ascontiguousarray(bq.reshape(HT, 128).T)
    bk_col = np.ascontiguousarray(bk.reshape(HT, 128).T)
    ident = np.eye(128, dtype=ml_dtypes.bfloat16)

    j = np.arange(T, dtype=np.float64)
    gp = (GAMMA ** j).astype(ml_dtypes.bfloat16)
    gp_bc = np.ascontiguousarray(np.broadcast_to(gp[None, :], (128, T)))
    wown_col = np.ascontiguousarray(
        (GAMMA ** (-j)).astype(np.float32).reshape(T // 128, 128).T)
    wcore1 = np.full((128, 1), GAMMA ** float(T), dtype=np.float32)
    wcore0 = np.zeros((128, 1), dtype=np.float32)
    u = np.arange(128, dtype=np.float64)
    maskT = np.zeros((128, CT * C), dtype=ml_dtypes.bfloat16)
    for ut in range(CT):
        uu = ut * 128 + u[:, None]
        rr = np.arange(C, dtype=np.float64)[None, :]
        maskT[:, ut * C:(ut + 1) * C] = np.where(
            rr >= uu, GAMMA ** np.maximum(rr - uu, 0.0), 0.0).astype(ml_dtypes.bfloat16)

    nc = _get_nc()
    in_maps = []
    for core in range(8):
        b, half = core // 2, core % 2
        s0 = half * T
        xTc = np.ascontiguousarray(x[b, s0:s0 + T, :].T)
        in_maps.append({
            "xT": xTc,
            "WqT": WqT, "WkT": WkT, "WvT": WvT,
            "bq_col": bq_col, "bk_col": bk_col, "ident": ident,
            "maskT": maskT, "gp_bc": gp_bc,
            "wown_col": wown_col,
            "wcore_col": wcore1 if half == 1 else wcore0,
            "WgT": WgT, "WuT": WuT, "WdT": WdT,
        })

    res = run_bass_kernel_spmd(nc, in_maps, core_ids=list(range(8)))

    out = np.empty((4, 4096, H), dtype=np.float32)
    for core in range(8):
        b, half = core // 2, core % 2
        s0 = half * T
        out[b, s0:s0 + T, :] = res.results[core]["out"].T
    return out


# version 19 (final)
# speedup vs baseline: 1.0161x; 1.0161x over previous
import numpy as np
import ml_dtypes
from contextlib import ExitStack

import concourse.bass as bass
import concourse.mybir as mybir
from concourse.bass_utils import run_bass_kernel_spmd

F32 = mybir.dt.float32
BF16 = mybir.dt.bfloat16
I8 = mybir.dt.int8
AF = mybir.ActivationFunctionType
ALU = mybir.AluOpType

H = 1024
T = 2048
C = 256
NCH = T // C
CT = C // 128
HT = H // 128
FF = 3072
FS = 1024
NS = FF // FS
FT = FS // 128
TC = 512
GAMMA = 0.96875
SCL = 1.0 / 127.0
EPS = 1e-5

ENGINES = ("tensor", "vector", "scalar", "sync", "gpsimd")


class Prog:

    def __init__(self, nc):
        self.nc = nc
        self.streams = {e: [] for e in ENGINES}
        self.sems = {}
        self.counts = {}
        self.waited = {e: {} for e in ENGINES}

    def add_sem(self, name, handle):
        self.sems[name] = handle
        self.counts[name] = 0

    def emit(self, eng, fn, waits=(), inc=None, inc_n=1):
        real = []
        for tok in waits:
            if tok is None:
                continue
            s, v = tok
            if v <= self.waited[eng].get(s, 0):
                continue
            self.waited[eng][s] = v
            real.append((s, v))
        tok = None
        if inc is not None:
            self.counts[inc] += inc_n
            tok = (inc, self.counts[inc])
        self.streams[eng].append((real, fn, inc, inc_n))
        return tok

    def replay(self, block):
        def mk(eng):
            ops = self.streams[eng]

            def run(e):
                for waits, fn, inc, inc_n in ops:
                    for s, v in waits:
                        e.wait_ge(self.sems[s], v)
                    inst = fn(e)
                    if inc is not None:
                        inst.then_inc(self.sems[inc], inc_n)
            return run

        for eng in ENGINES:
            if self.streams[eng]:
                getattr(block, eng)(mk(eng))
        self.streams = {e: [] for e in ENGINES}


class PsumPool:
    def __init__(self, banks):
        self.banks = banks
        self.tok = [None] * len(banks)
        self.i = 0

    def acquire(self):
        b = self.i % len(self.banks)
        self.i += 1
        return b, self.banks[b], self.tok[b]

    def release(self, b, tok):
        self.tok[b] = tok


def build_nc():
    nc = bass.Bass("TRN2", target_bir_lowering=False, dynamic_dma_scratch_size=16384)
    P = Prog(nc)

    xT_d = nc.dram_tensor("xT", [H, T], F32, kind="ExternalInput")
    WqT_d = nc.dram_tensor("WqT", [H, H], F32, kind="ExternalInput")
    WkT_d = nc.dram_tensor("WkT", [H, H], F32, kind="ExternalInput")
    WvT_d = nc.dram_tensor("WvT", [H, H], F32, kind="ExternalInput")
    bq_col_d = nc.dram_tensor("bq_col", [128, HT], F32, kind="ExternalInput")
    bk_col_d = nc.dram_tensor("bk_col", [128, HT], F32, kind="ExternalInput")
    maskT_d = nc.dram_tensor("maskT", [128, CT * C], BF16, kind="ExternalInput")
    gp_d = nc.dram_tensor("gp_bc", [128, T], BF16, kind="ExternalInput")
    wown_d = nc.dram_tensor("wown_col", [128, T // 128], F32, kind="ExternalInput")
    wcore_d = nc.dram_tensor("wcore_col", [128, 1], F32, kind="ExternalInput")
    ident_d = nc.dram_tensor("ident", [128, 128], BF16, kind="ExternalInput")
    WgT_d = nc.dram_tensor("WgT", [H, FF], I8, kind="ExternalInput")
    WuT_d = nc.dram_tensor("WuT", [H, FF], I8, kind="ExternalInput")
    WdT_d = nc.dram_tensor("WdT", [FF, H], I8, kind="ExternalInput")
    out_d = nc.dram_tensor("out", [H, T], F32, kind="ExternalOutput")
    cc_in_d = nc.dram_tensor("cc_in", [128, HT * H], BF16, kind="Internal")
    cc_out_d = nc.dram_tensor("cc_out", [256, HT * H], BF16, kind="Internal")
    cc2_in_d = nc.dram_tensor("cc2_in", [128, 2 * CT * H], BF16, kind="Internal")
    cc2_out_d = nc.dram_tensor("cc2_out", [256, 2 * CT * H], BF16, kind="Internal")
    cc0_in_d = nc.dram_tensor("cc0_in", [1, 16], F32, kind="Internal")
    cc0_out_d = nc.dram_tensor("cc0_out", [8, 16], F32, kind="Internal")

    with ExitStack() as ctx:
        E = ctx.enter_context
        for s in ("sT", "sV", "sS", "sG", "sY", "dX", "dA", "dWkv", "dWq", "dCC",
                  "dCi", "dCk", "dCv", "dC2", "dC2b", "dF0", "dF1", "dWd", "dO"):
            P.add_sem(s, E(nc.semaphore(s)))

        xT = E(nc.sbuf_tensor("xT_sb", [128, HT * T], F32))
        ones_col = E(nc.sbuf_tensor("ones_col", [128, 1], BF16))
        ones_row = E(nc.sbuf_tensor("ones_row", [1, 128], BF16))
        eps_sb = E(nc.sbuf_tensor("eps_sb", [1, 1], F32))
        rows_own = {}
        rows_own["rstd_bf"] = E(nc.sbuf_tensor("ro_rstd_bf", [1, T], BF16))
        rows_own["mr_bf"] = E(nc.sbuf_tensor("ro_mr_bf", [1, T], BF16))
        rbc = E(nc.sbuf_tensor("rbc", [128, C], BF16))
        mrbc = E(nc.sbuf_tensor("mrbc", [128, C], BF16))
        psum = [E(nc.psum_tensor(f"ps{i}", [128, 512], F32)) for i in range(7)]
        pool = PsumPool(psum)
        pst = E(nc.psum_tensor("pst", [128, 1024], BF16))
        cast_t = sqr_t = rows_pro = None

        tok_consts = None
        stats_guard = {}
        bc_guard = {}

        def emit_stats_chunk(src_of, o0, w, rows, waits):
            bs, bank_s, busy_s = pool.acquire()
            bqi, bank_q, busy_q = pool.acquire()
            tok_sum = tok_sq = None
            for i in range(HT):
                buf = cast_t[i % 2]
                prev = stats_guard.get(i % 2)
                src = src_of(i)
                ct = P.emit("scalar",
                            lambda e, o=buf[:, :w], s=src: e.copy(o, s),
                            waits=list(waits) + [prev], inc="sS")
                tok_sum = P.emit(
                    "tensor",
                    lambda e, o=bank_s[:1, :w], r=buf[:, :w], s1=(i == 0), s2=(i == HT - 1):
                    e.matmul(o, ones_col[:, :1], r, start=s1, stop=s2),
                    waits=([ct, busy_s, tok_consts] if i == 0 else [ct]),
                    inc="sT")
                st = P.emit("gpsimd",
                            lambda e, o=buf[:, :w]: e.tensor_mul(o, o, o),
                            waits=[tok_sum], inc="sG")
                tok_sq = P.emit(
                    "tensor",
                    lambda e, o=bank_q[:1, :w], r=buf[:, :w], s1=(i == 0), s2=(i == HT - 1):
                    e.matmul(o, ones_col[:, :1], r, start=s1, stop=s2),
                    waits=([st, busy_q] if i == 0 else [st]),
                    inc="sT")
                stats_guard[i % 2] = tok_sq
            t1 = P.emit("vector",
                        lambda e, o=rows["m"][:1, o0:o0 + w], s=bank_s[:1, :w]:
                        e.tensor_scalar_mul(o, s, 1.0 / H),
                        waits=[tok_sum, tok_sq], inc="sV")
            t2 = P.emit("vector",
                        lambda e, o=rows["var"][:1, o0:o0 + w], a=rows["m"][:1, o0:o0 + w]:
                        e.tensor_mul(o, a, a), waits=[t1], inc="sV")
            t3 = P.emit("vector",
                        lambda e, o=rows["var"][:1, o0:o0 + w], s=bank_q[:1, :w],
                        b=rows["var"][:1, o0:o0 + w]:
                        e.scalar_tensor_tensor(o, s, 1.0 / H, b, ALU.mult, ALU.subtract),
                        waits=[t2], inc="sV")
            pool.release(bs, t1)
            pool.release(bqi, t3)
            return t3

        def emit_rows_finalize(rows, n_tok, rows_tok, scratch=None):
            mid = scratch if scratch is not None else rows["var"][:1, :n_tok]
            a1 = P.emit("scalar",
                        lambda e, o=mid, i_=rows["var"][:1, :n_tok]:
                        e.activation(o, i_, AF.Ln, bias=eps_sb[:1, :1], scale=1.0),
                        waits=[rows_tok, tok_consts], inc="sS")
            a2 = P.emit("scalar",
                        lambda e, o=rows["var"][:1, :n_tok], i_=mid:
                        e.activation(o, i_, AF.Exp, scale=-0.5), waits=[a1], inc="sS")
            b1 = P.emit("vector",
                        lambda e, o=rows["rstd_bf"][:1, :n_tok], i_=rows["var"][:1, :n_tok]:
                        e.tensor_copy(o, i_), waits=[a2], inc="sV")
            b2 = P.emit("vector",
                        lambda e, o=rows["m"][:1, :n_tok], a=rows["m"][:1, :n_tok],
                        b=rows["var"][:1, :n_tok]: e.tensor_mul(o, a, b),
                        waits=[a2, b1], inc="sV")
            b3 = P.emit("vector",
                        lambda e, o=rows["mr_bf"][:1, :n_tok], i_=rows["m"][:1, :n_tok]:
                        e.tensor_copy(o, i_), waits=[b2], inc="sV")
            return b3

        def emit_ln_apply(src_of, dst_of, cw, rows, roff, rows_tok, extra_waits=(),
                          bc=None):
            rbc_, mrbc_ = bc if bc is not None else (rbc, mrbc)
            br, bank_r, busy_r = pool.acquire()
            tr = P.emit("tensor",
                        lambda e, o=bank_r[:128, :cw], r=rows["rstd_bf"][:1, roff:roff + cw]:
                        e.matmul(o, ones_row[:1, :128], r, start=True, stop=True),
                        waits=[rows_tok, busy_r, tok_consts, tok_dA], inc="sT")
            bm, bank_m, busy_m = pool.acquire()
            tm = P.emit("tensor",
                        lambda e, o=bank_m[:128, :cw], r=rows["mr_bf"][:1, roff:roff + cw]:
                        e.matmul(o, ones_row[:1, :128], r, start=True, stop=True),
                        waits=[busy_m], inc="sT")
            e1 = P.emit("vector", lambda e, o=rbc_[:, :cw], s=bank_r[:128, :cw]:
                        e.tensor_copy(o, s),
                        waits=[tr, bc_guard.get(id(rbc_))], inc="sV")
            e2 = P.emit("vector", lambda e, o=mrbc_[:, :cw], s=bank_m[:128, :cw]:
                        e.tensor_copy(o, s),
                        waits=[tm, bc_guard.get(id(rbc_))], inc="sV")
            pool.release(br, e1)
            pool.release(bm, e2)
            tok = None
            for i in range(HT):
                tma = P.emit("vector",
                             lambda e, o=dst_of(i), s=src_of(i), m=rbc_[:, :cw]:
                             e.tensor_mul(o, s, m),
                             waits=list(extra_waits) + [e1, e2], inc="sV")
                tok = P.emit("vector",
                             lambda e, o=dst_of(i), m=mrbc_[:, :cw]:
                             e.tensor_sub(o, o, m), waits=[tma], inc="sV")
            bc_guard[id(rbc_)] = tok
            return tok

        with ExitStack() as mctx:
            M = mctx.enter_context
            cast_t = [M(nc.sbuf_tensor(f"castt{i}", [128, TC], BF16)) for i in range(2)]
            for n in ("m", "var"):
                rows_own[n] = M(nc.sbuf_tensor(f"ro_{n}", [1, T], BF16))
            qsc_all = M(nc.sbuf_tensor("qsc_all", [128, HT * T], BF16))
            S_sb = M(nc.sbuf_tensor("S_sb", [128, HT * H], BF16))
            wcore = M(nc.sbuf_tensor("wcore_sb", [128, 1], F32))
            gp_bc = M(nc.sbuf_tensor("gp_sb", [128, T], BF16))
            actx = mctx.enter_context(ExitStack())
            A = actx.enter_context
            WqT = A(nc.sbuf_tensor("WqT_sb", [128, HT * H], BF16))
            WkT = A(nc.sbuf_tensor("WkT_sb", [128, HT * H], BF16))
            WvT = A(nc.sbuf_tensor("WvT_sb", [128, HT * H], BF16))
            bq_col = A(nc.sbuf_tensor("bq_sb", [128, HT], F32))
            bk_col = A(nc.sbuf_tensor("bkc_sb", [128, HT], F32))
            maskT = A(nc.sbuf_tensor("maskT_sb", [128, CT * C], BF16))
            wown = A(nc.sbuf_tensor("wown_sb", [128, T // 128], F32))
            ident = A(nc.sbuf_tensor("ident_sb", [128, 128], BF16))
            lnT = [A(nc.sbuf_tensor(f"lnT{i}", [128, HT * C], BF16)) for i in range(2)]
            k_sb = A(nc.sbuf_tensor("k_sb", [128, HT * C], BF16))
            ktm = A(nc.sbuf_tensor("ktm_sb", [128, CT * H], BF16))
            v_sb = A(nc.sbuf_tensor("v_sb", [128, CT * H], BF16))
            scT = A(nc.sbuf_tensor("scT_sb", [128, CT * C], BF16))

            for i in range(HT):
                P.emit("gpsimd", lambda e, i=i: e.dma_start(
                    xT[:, i * T:(i + 1) * T], xT_d.ap()[i * 128:(i + 1) * 128, :]),
                    inc="dX", inc_n=16)
            for dst, src in ((maskT, maskT_d), (gp_bc, gp_d), (wown, wown_d),
                             (wcore, wcore_d), (bq_col, bq_col_d), (bk_col, bk_col_d),
                             (ident, ident_d)):
                P.emit("sync", lambda e, o=dst, s=src: e.dma_start(o[:], s.ap()),
                       inc="dA", inc_n=16)
            tok_dA = ("dA", 16 * 7)
            tok_dX = ("dX", 16 * 8)

            for i in range(HT):
                P.emit("gpsimd", lambda e, i=i: e.dma_start(
                    WqT[:, i * H:(i + 1) * H], WqT_d.ap()[i * 128:(i + 1) * 128, :]),
                    inc="dWq", inc_n=16)
            tok_dWq = ("dWq", 16 * 8)
            for W_sb, W_d in ((WkT, WkT_d), (WvT, WvT_d)):
                for i in range(HT):
                    P.emit("gpsimd", lambda e, o=W_sb, d=W_d, i=i: e.dma_start(
                        o[:, i * H:(i + 1) * H], d.ap()[i * 128:(i + 1) * 128, :]),
                        inc="dWkv", inc_n=16)
            tok_dWkv = ("dWkv", 16 * 16)

            P.emit("vector", lambda e: e.memset(ones_col[:], 1.0))
            P.emit("vector", lambda e: e.memset(ones_row[:1, :], 1.0))
            P.emit("vector", lambda e: e.memset(eps_sb[:1, :1], EPS))
            tok_consts = P.emit("vector", lambda e: e.memset(S_sb[:], 0.0), inc="sV")

            S_tok = [tok_consts]
            cross_done = []
            pst_busy = [None]

            def proj_fm(W_sb, out_of, b_col, lnb, ln_tok, w_tok):
                tok = None
                for mo in range(HT):
                    bb, bank, busy = pool.acquire()
                    for i in range(HT):
                        w = [ln_tok, w_tok, busy, tok_dX, tok_dA] if i == 0 else ()
                        P.emit("tensor",
                               lambda e, o=bank[:128, :C],
                               a=W_sb[:, i * H + mo * 128: i * H + mo * 128 + 128],
                               r=lnb[:, i * C:(i + 1) * C], s1=(i == 0), s2=(i == HT - 1):
                               e.matmul(o, a, r, start=s1, stop=s2),
                               waits=w, inc=("sT" if i == HT - 1 else None))
                    gtok = ("sT", P.counts["sT"])
                    tok = P.emit("scalar",
                                 lambda e, o=out_of(mo),
                                 s=bank[:128, :C], b=b_col[:, mo:mo + 1]:
                                 e.activation(o, s, AF.Identity, bias=b, scale=1.0),
                                 waits=[gtok, tok_dA], inc="sS")
                    pool.release(bb, tok)
                return tok

            def proj_tm(W_sb, out_sb, scale_of, bias_row, lnb, ln_tok, w_tok):
                tok = None
                for mt in range(CT):
                    for no in range(H // 512):
                        bb, bank, busy = pool.acquire()
                        npairs = HT + (1 if bias_row is not None else 0)
                        for i in range(HT):
                            w = [ln_tok, w_tok, busy, tok_dX, tok_dA] if i == 0 else ()
                            P.emit("tensor",
                                   lambda e, o=bank[:128, :512],
                                   a=lnb[:, i * C + mt * 128: i * C + mt * 128 + 128],
                                   r=W_sb[:, i * H + no * 512: i * H + (no + 1) * 512],
                                   s1=(i == 0), s2=(i == npairs - 1):
                                   e.matmul(o, a, r, start=s1, stop=s2),
                                   waits=w, inc=("sT" if i == npairs - 1 else None))
                        gtok = ("sT", P.counts["sT"])
                        dst = out_sb[:, mt * H + no * 512: mt * H + (no + 1) * 512]
                        if scale_of is not None:
                            sc = scale_of(mt)
                            tok = P.emit("vector",
                                         lambda e, o=dst, s=bank[:128, :512], c=sc:
                                         e.tensor_scalar(o, s, c, None, ALU.mult),
                                         waits=[gtok, tok_dA], inc="sV")
                        else:
                            tok = P.emit("scalar",
                                         lambda e, o=dst, s=bank[:128, :512]: e.copy(o, s),
                                         waits=[gtok], inc="sS")
                        pool.release(bb, tok)
                return tok

            def emit_state_update(v_tok, ktm_tok):
                nonlocal S_tok, cross_done
                new_tok = []
                for nh in (1, 0):
                    for mo in range(HT):
                        bb, bank, busy = pool.acquire()
                        for j in range(CT):
                            w = [v_tok, ktm_tok, busy] if j == 0 else ()
                            P.emit("tensor",
                                   lambda e, o=bank[:128, :512],
                                   a=ktm[:, j * H + mo * 128: j * H + mo * 128 + 128],
                                   r=v_sb[:, j * H + nh * 512: j * H + (nh + 1) * 512],
                                   s1=(j == 0), s2=(j == CT - 1):
                                   e.matmul(o, a, r, start=s1, stop=s2),
                                   waits=w, inc=("sT" if j == CT - 1 else None))
                        gtok = ("sT", P.counts["sT"])
                        sl = S_sb[:, mo * H + nh * 512: mo * H + (nh + 1) * 512]
                        cd = (cross_done[4 * nh:4 * nh + 4]
                              if len(cross_done) == HT else cross_done)
                        ut = P.emit("vector",
                                    lambda e, o=sl, s=bank[:128, :512]: e.tensor_add(o, o, s),
                                    waits=[gtok] + cd + S_tok, inc="sV")
                        pool.release(bb, ut)
                        new_tok.append(ut)
                S_tok = new_tok
                cross_done = []

            def emit_chunk(c_idx, lnb, ln_tok, mid_hook=None, do_update=True):
                nonlocal cross_done
                res_toks = []
                qof = lambda mo, c=c_idx: qsc_all[:, mo * T + c * C: mo * T + (c + 1) * C]
                q_tok = proj_fm(WqT, qof, bq_col, lnb, ln_tok, tok_dWq)
                k_tok = proj_fm(WkT, lambda mo: k_sb[:, mo * C:(mo + 1) * C],
                                bk_col, lnb, ln_tok, tok_dWkv)
                ktm_tok = None
                for mt in range(CT):
                    for i in range(HT):
                        w = [k_tok, pst_busy[0], tok_dA] if i == 0 else ()
                        P.emit("tensor",
                               lambda e, o=pst[:, i * 128:(i + 1) * 128],
                               s=k_sb[:, i * C + mt * 128: i * C + (mt + 1) * 128]:
                               e.transpose(o, s, ident[:]),
                               waits=w, inc=("sT" if i == HT - 1 else None))
                    gtok = ("sT", P.counts["sT"])
                    ktm_tok = P.emit("vector",
                                     lambda e, o=ktm[:, mt * H:(mt + 1) * H],
                                     s=pst[:, :], c=wown[:, c_idx * CT + mt: c_idx * CT + mt + 1]:
                                     e.tensor_scalar(o, s, c, None, ALU.mult),
                                     waits=[gtok], inc="sV")
                    pst_busy[0] = ktm_tok
                v_tok = proj_tm(WvT, v_sb, None, None, lnb, ln_tok, tok_dWkv)

                sc_mm, sc_ev = [], []
                for mj in range(CT):
                    bb, bank, busy = pool.acquire()
                    for i in range(HT):
                        w = [q_tok, k_tok, busy] if i == 0 else ()
                        P.emit("tensor",
                               lambda e, o=bank[:128, :C],
                               a=k_sb[:, i * C + mj * 128: i * C + mj * 128 + 128],
                               r=qof(i),
                               s1=(i == 0), s2=(i == HT - 1):
                               e.matmul(o, a, r, start=s1, stop=s2),
                               waits=w, inc=("sT" if i == HT - 1 else None))
                    gtok = ("sT", P.counts["sT"])
                    sc_mm.append(gtok)
                    ev = P.emit("vector",
                                lambda e, o=scT[:, mj * C:(mj + 1) * C],
                                s=bank[:128, :C], m=maskT[:, mj * C:(mj + 1) * C]:
                                e.tensor_mul(o, s, m),
                                waits=[gtok, tok_dA], inc="sV")
                    sc_ev.append(ev)
                    pool.release(bb, ev)
                qsc_tok = None
                for mo in range(HT):
                    qsc_tok = P.emit("vector",
                                     lambda e, o=qof(mo),
                                     g=gp_bc[:, c_idx * C:(c_idx + 1) * C]:
                                     e.tensor_mul(o, o, g),
                                     waits=sc_mm + [q_tok, tok_dA], inc="sV")
                if mid_hook is not None:
                    mid_hook()
                for mh in range(HT):
                    bb, bank, busy = pool.acquire()
                    pairs = [(v_sb[:, j * H + mh * 128: j * H + mh * 128 + 128],
                              scT[:, j * C:(j + 1) * C]) for j in range(CT)]
                    pairs += [(S_sb[:, i * H + mh * 128: i * H + mh * 128 + 128],
                               qof(i)) for i in range(HT)]
                    for idx, (a_, r_) in enumerate(pairs):
                        w = sc_ev + [qsc_tok, v_tok, busy] + S_tok if idx == 0 else ()
                        P.emit("tensor",
                               lambda e, o=bank[:128, :C], a=a_, r=r_,
                               s1=(idx == 0), s2=(idx == len(pairs) - 1):
                               e.matmul(o, a, r, start=s1, stop=s2),
                               waits=w, inc=("sT" if idx == len(pairs) - 1 else None))
                    gtok = ("sT", P.counts["sT"])
                    cross_done.append(gtok)
                    sl = xT[:, mh * T + c_idx * C: mh * T + (c_idx + 1) * C]
                    ev = P.emit("vector",
                                lambda e, o=sl, s=bank[:128, :C]: e.tensor_add(o, o, s),
                                waits=[gtok], inc="sV")
                    res_toks.append(ev)
                    pool.release(bb, ev)
                if do_update:
                    emit_state_update(v_tok, ktm_tok)
                return res_toks, qsc_tok, ktm_tok, v_tok

            t3 = None
            for ck in range(T // TC):
                t3 = emit_stats_chunk(
                    lambda i, ck=ck: xT[:, i * T + ck * TC: i * T + (ck + 1) * TC],
                    ck * TC, TC, rows_own, waits=[tok_dX, tok_dA])
            rows1_tok = emit_rows_finalize(rows_own, T, t3)

            def prepare_unit(u):
                return emit_ln_apply(
                    lambda i, c=u: xT[:, i * T + c * C: i * T + (c + 1) * C],
                    lambda i, p=u % 2: lnT[p][:, i * C:(i + 1) * C], C,
                    rows_own, u * C, rows1_tok)

            ln_tok = prepare_unit(0)
            res_by_chunk = {}
            qsc_last = None
            for u in range(NCH):
                nxt = {}

                def mid_hook(u=u, nxt=nxt):
                    if u + 1 < NCH:
                        nxt["tok"] = prepare_unit(u + 1)

                if u == NCH - 1:
                    P.emit("gpsimd", lambda e: e.dma_start(cc_in_d.ap(), S_sb[:]),
                           waits=S_tok, inc="dCC", inc_n=16)
                res, qsc_last, ktm7_tok, v7_tok = emit_chunk(
                    u, lnb=lnT[u % 2], ln_tok=ln_tok, mid_hook=mid_hook,
                    do_update=(u != NCH - 1))
                res_by_chunk[u] = res
                ln_tok = nxt.get("tok")

            P.emit("sync", lambda e: e.dma_start(cc2_in_d.ap()[:, 0:CT * H], ktm[:]),
                   waits=[ktm7_tok], inc="dC2", inc_n=16)
            P.emit("sync", lambda e: e.dma_start(cc2_in_d.ap()[:, CT * H:], v_sb[:]),
                   waits=[v7_tok], inc="dC2b", inc_n=16)
            P.emit("gpsimd", lambda e: e.collective_compute(
                "AllGather", ALU.bypass,
                ins=[cc_in_d.ap().opt()], outs=[cc_out_d.ap().opt()],
                replica_groups=[[0, 1], [2, 3], [4, 5], [6, 7]]),
                waits=[("dCC", 16)], inc="sG")
            tok_cc1 = ("sG", P.counts["sG"])
            P.emit("gpsimd", lambda e: e.collective_compute(
                "AllGather", ALU.bypass,
                ins=[cc2_in_d.ap().opt()], outs=[cc2_out_d.ap().opt()],
                replica_groups=[[0, 1], [2, 3], [4, 5], [6, 7]]),
                waits=[("dC2", 16), ("dC2b", 16)], inc="sG")
            tok_cc2 = ("sG", P.counts["sG"])
            P.emit("gpsimd", lambda e: e.dma_start(S_sb[:], cc_out_d.ap()[0:128, :]),
                   waits=[tok_cc1, cross_done[-1]], inc="dCi", inc_n=16)
            P.emit("gpsimd", lambda e: e.dma_start(ktm[:], cc2_out_d.ap()[0:128, 0:CT * H]),
                   waits=[tok_cc2], inc="dCk", inc_n=16)
            P.emit("gpsimd", lambda e: e.dma_start(v_sb[:], cc2_out_d.ap()[0:128, CT * H:]),
                   inc="dCv", inc_n=16)
            tok_Sscale = P.emit("vector",
                                lambda e: e.tensor_scalar(S_sb[:], S_sb[:],
                                                          wcore[:, :1], None, ALU.mult),
                                waits=[("dCi", 16), tok_dA], inc="sV")
            tok_Sinit = None
            for nh in range(H // 512):
                for mo in range(HT):
                    bb, bank, busy = pool.acquire()
                    for j in range(CT):
                        w = [("dCk", 16), ("dCv", 16), busy] if j == 0 else ()
                        P.emit("tensor",
                               lambda e, o=bank[:128, :512],
                               a=ktm[:, j * H + mo * 128: j * H + mo * 128 + 128],
                               r=v_sb[:, j * H + nh * 512: j * H + (nh + 1) * 512],
                               s1=(j == 0), s2=(j == CT - 1):
                               e.matmul(o, a, r, start=s1, stop=s2),
                               waits=w, inc=("sT" if j == CT - 1 else None))
                    gtok = ("sT", P.counts["sT"])
                    sl = S_sb[:, mo * H + nh * 512: mo * H + (nh + 1) * 512]
                    tok_Sinit = P.emit("vector",
                                       lambda e, o=sl, s=bank[:128, :512]:
                                       e.scalar_tensor_tensor(o, s, wcore[:, :1], o,
                                                              ALU.mult, ALU.add),
                                       waits=[gtok, tok_Sscale], inc="sV")
                    pool.release(bb, tok_Sinit)

            with nc.Block() as blockA1:
                P.replay(blockA1)
            actx.close()
            wg_sb = [E(nc.sbuf_tensor("wg0", [128, HT * FS], BF16, side="right")), None]
            wu_sb = [E(nc.sbuf_tensor("wu0", [128, HT * FS], BF16, side="right")), None]
            wd_sb = E(nc.sbuf_tensor("wd0", [128, FT * H], BF16, side="right"))

            def stripe_dma(s, gate_tok):
                par = s % 2
                P.emit("gpsimd",
                       lambda e, s=s, p=par: e.dma_start(
                           bass.AP(wg_sb[p], 0, [[HT * FS, 128], [FS, HT], [1, FS]]),
                           bass.AP(WgT_d, s * FS, [[FF, 128], [128 * FF, HT], [1, FS]])),
                       waits=[gate_tok], inc=f"dF{par}", inc_n=16)
                P.emit("gpsimd",
                       lambda e, s=s, p=par: e.dma_start(
                           bass.AP(wu_sb[p], 0, [[HT * FS, 128], [FS, HT], [1, FS]]),
                           bass.AP(WuT_d, s * FS, [[FF, 128], [128 * FF, HT], [1, FS]])),
                       inc=f"dF{par}", inc_n=16)
                return (f"dF{par}", P.counts[f"dF{par}"])

            def wd_dma(s, gate_tok):
                P.emit("gpsimd",
                       lambda e, s=s: e.dma_start(
                           bass.AP(wd_sb, 0, [[FT * H, 128], [H, FT], [1, H]]),
                           bass.AP(WdT_d, s * FS * H, [[H, 128], [128 * H, FT], [1, H]])),
                       waits=[gate_tok], inc="dWd", inc_n=16)
                return ("dWd", P.counts["dWd"])

            tok_F = [stripe_dma(0, None), None, None]
            tok_Fd = wd_dma(0, None)

            ln2_stat_tok = None
            for tc in range(T // TC):
                pb_toks = []
                for mh in range(HT):
                    bb, bank, busy = pool.acquire()
                    for i in range(HT):
                        w = [tok_Sinit, qsc_last, busy] if i == 0 else ()
                        P.emit("tensor",
                               lambda e, o=bank[:128, :512],
                               a=S_sb[:, i * H + mh * 128: i * H + mh * 128 + 128],
                               r=qsc_all[:, i * T + tc * TC: i * T + (tc + 1) * TC],
                               s1=(i == 0), s2=(i == HT - 1):
                               e.matmul(o, a, r, start=s1, stop=s2),
                               waits=w, inc=("sT" if i == HT - 1 else None))
                    gtok = ("sT", P.counts["sT"])
                    sl = xT[:, mh * T + tc * TC: mh * T + (tc + 1) * TC]
                    ev = P.emit("vector",
                                lambda e, o=sl, s=bank[:128, :512]: e.tensor_add(o, o, s),
                                waits=[gtok], inc="sV")
                    pb_toks.append(ev)
                    pool.release(bb, ev)
                ln2_stat_tok = emit_stats_chunk(
                    lambda i, tc=tc: xT[:, i * T + tc * TC: i * T + (tc + 1) * TC],
                    tc * TC, TC, rows_own, waits=pb_toks)
            rows2_tok = emit_rows_finalize(
                rows_own, T, ln2_stat_tok,
                scratch=S_sb[0:1, :].bitcast(F32)[:1, :T])

            with nc.Block() as blockA2:
                P.replay(blockA2)

        with ExitStack() as fctx:
            B = fctx.enter_context
            ln2T = B(nc.sbuf_tensor("ln2T_sb", [128, HT * T], BF16))
            wg_sb[1] = B(nc.sbuf_tensor("wg1", [128, HT * FS], BF16))
            wu_sb[1] = B(nc.sbuf_tensor("wu1", [128, HT * FS], BF16))
            gsil = B(nc.sbuf_tensor("gsil_sb", [128, FT * TC], BF16))
            h_sb = B(nc.sbuf_tensor("h_sb", [128, FT * TC], BF16))
            rbc_b = B(nc.sbuf_tensor("rbc_b", [128, TC], BF16))
            mrbc_b = B(nc.sbuf_tensor("mrbc_b", [128, TC], BF16))

            tok_F[1] = stripe_dma(1, None)
            ln2_toks = []
            for ck in range(T // TC):
                ln2_toks.append(emit_ln_apply(
                    lambda i, ck=ck: xT[:, i * T + ck * TC: i * T + ck * TC + TC],
                    lambda i, ck=ck: ln2T[:, i * T + ck * TC: i * T + ck * TC + TC],
                    TC, rows_own, ck * TC, rows2_tok, bc=(rbc_b, mrbc_b)))

            h_prev_tok = None
            for s in range(NS):
                par = s % 2
                wg, wu, wd = wg_sb[par], wu_sb[par], wd_sb
                last_w_consumer = None
                for tc in range(T // TC):
                    gs_toks = []
                    for mf in range(FT):
                        bb, bank, busy = pool.acquire()
                        for i in range(HT):
                            w = [ln2_toks[tc], tok_F[s], busy] if i == 0 else ()
                            P.emit("tensor",
                                   lambda e, o=bank[:128, :TC],
                                   a=wg[:, i * FS + mf * 128: i * FS + mf * 128 + 128],
                                   r=ln2T[:, i * T + tc * TC: i * T + (tc + 1) * TC],
                                   s1=(i == 0), s2=(i == HT - 1):
                                   e.matmul(o, a, r, start=s1, stop=s2),
                                   waits=w, inc=("sT" if i == HT - 1 else None))
                        gtok = ("sT", P.counts["sT"])
                        ev = P.emit("scalar",
                                    lambda e, o=gsil[:, mf * TC:(mf + 1) * TC],
                                    s=bank[:128, :TC]:
                                    e.activation(o, s, AF.Silu, scale=SCL),
                                    waits=[gtok, h_prev_tok], inc="sS")
                        gs_toks.append(ev)
                        pool.release(bb, ev)
                    h_toks = []
                    for mf in range(FT):
                        bb, bank, busy = pool.acquire()
                        for i in range(HT):
                            w = [ln2_toks[tc], tok_F[s], busy] if i == 0 else ()
                            P.emit("tensor",
                                   lambda e, o=bank[:128, :TC],
                                   a=wu[:, i * FS + mf * 128: i * FS + mf * 128 + 128],
                                   r=ln2T[:, i * T + tc * TC: i * T + (tc + 1) * TC],
                                   s1=(i == 0), s2=(i == HT - 1):
                                   e.matmul(o, a, r, start=s1, stop=s2),
                                   waits=w, inc=("sT" if i == HT - 1 else None))
                        gtok = ("sT", P.counts["sT"])
                        ht = P.emit("vector",
                                    lambda e, o=h_sb[:, mf * TC:(mf + 1) * TC],
                                    s=bank[:128, :TC], g=gsil[:, mf * TC:(mf + 1) * TC]:
                                    e.scalar_tensor_tensor(o, s, SCL, g, ALU.mult, ALU.mult),
                                    waits=[gtok, gs_toks[mf]], inc="sV")
                        h_toks.append(ht)
                        pool.release(bb, ht)
                    h_prev_tok = h_toks[-1]
                    for md in range(HT):
                        bb, bank, busy = pool.acquire()
                        for j in range(FT):
                            w = h_toks + [busy, tok_Fd] if j == 0 else ()
                            P.emit("tensor",
                                   lambda e, o=bank[:128, :TC],
                                   a=wd[:, j * H + md * 128: j * H + md * 128 + 128],
                                   r=h_sb[:, j * TC:(j + 1) * TC],
                                   s1=(j == 0), s2=(j == FT - 1):
                                   e.matmul(o, a, r, start=s1, stop=s2),
                                   waits=w, inc=("sT" if j == FT - 1 else None))
                        gtok = ("sT", P.counts["sT"])
                        sl = xT[:, md * T + tc * TC: md * T + (tc + 1) * TC]
                        ev = P.emit("vector",
                                    lambda e, o=sl, s=bank[:128, :TC]:
                                    e.scalar_tensor_tensor(o, s, SCL, o, ALU.mult, ALU.add),
                                    waits=[gtok], inc="sV")
                        pool.release(bb, ev)
                        last_w_consumer = gtok
                        if s == NS - 1:
                            P.emit("sync", lambda e, md=md, tc=tc: e.dma_start(
                                out_d.ap()[md * 128:(md + 1) * 128,
                                           tc * TC:(tc + 1) * TC],
                                xT[:, md * T + tc * TC: md * T + (tc + 1) * TC]),
                                waits=[ev], inc="dO", inc_n=16)
                if s + 2 < NS:
                    tok_F[s + 2] = stripe_dma(s + 2, last_w_consumer)
                if s + 1 < NS:
                    tok_Fd = wd_dma(s + 1, last_w_consumer)

            with nc.Block() as blockB:
                P.replay(blockB)

        P.emit("sync", lambda e: e.nop(), waits=[("dO", 16 * HT * (T // TC))])

        with nc.Block() as blockC:
            P.replay(blockC)

    return nc



_NC_CACHE = {}


def _get_nc():
    if "nc" not in _NC_CACHE:
        _NC_CACHE["nc"] = build_nc()
    return _NC_CACHE["nc"]


def _int8_view(w_int32):
    return np.ascontiguousarray(w_int32.view(np.int8)[..., 0::4])


def kernel(L0_attn_x, L0_attn_Wq, L0_attn_bq, L0_attn_Wk, L0_attn_bk, L0_attn_Wv,
           L0_ffn_w_gate, L0_ffn_w_up, L0_ffn_w_down):
    x = np.ascontiguousarray(np.asarray(L0_attn_x, dtype=np.float32))
    Wq = np.asarray(L0_attn_Wq, dtype=np.float32)
    Wk = np.asarray(L0_attn_Wk, dtype=np.float32)
    Wv = np.asarray(L0_attn_Wv, dtype=np.float32)
    bq = np.asarray(L0_attn_bq, dtype=np.float32)
    bk = np.asarray(L0_attn_bk, dtype=np.float32)
    wg8 = _int8_view(np.asarray(L0_ffn_w_gate, dtype=np.int32))
    wu8 = _int8_view(np.asarray(L0_ffn_w_up, dtype=np.int32))
    wd8 = _int8_view(np.asarray(L0_ffn_w_down, dtype=np.int32))

    WqT = np.ascontiguousarray(Wq.T)
    WkT = np.ascontiguousarray(Wk.T)
    WvT = np.ascontiguousarray(Wv.T)
    WgT = np.ascontiguousarray(wg8.T)
    WuT = np.ascontiguousarray(wu8.T)
    WdT = np.ascontiguousarray(wd8.T)
    bq_col = np.ascontiguousarray(bq.reshape(HT, 128).T)
    bk_col = np.ascontiguousarray(bk.reshape(HT, 128).T)
    ident = np.eye(128, dtype=ml_dtypes.bfloat16)

    j = np.arange(T, dtype=np.float64)
    gp = (GAMMA ** j).astype(ml_dtypes.bfloat16)
    gp_bc = np.ascontiguousarray(np.broadcast_to(gp[None, :], (128, T)))
    wown_col = np.ascontiguousarray(
        (GAMMA ** (-j)).astype(np.float32).reshape(T // 128, 128).T)
    wcore1 = np.full((128, 1), GAMMA ** float(T), dtype=np.float32)
    wcore0 = np.zeros((128, 1), dtype=np.float32)
    u = np.arange(128, dtype=np.float64)
    maskT = np.zeros((128, CT * C), dtype=ml_dtypes.bfloat16)
    for ut in range(CT):
        uu = ut * 128 + u[:, None]
        rr = np.arange(C, dtype=np.float64)[None, :]
        maskT[:, ut * C:(ut + 1) * C] = np.where(
            rr >= uu, GAMMA ** np.maximum(rr - uu, 0.0), 0.0).astype(ml_dtypes.bfloat16)

    nc = _get_nc()
    in_maps = []
    for core in range(8):
        b, half = core // 2, core % 2
        s0 = half * T
        xTc = np.ascontiguousarray(x[b, s0:s0 + T, :].T)
        in_maps.append({
            "xT": xTc,
            "WqT": WqT, "WkT": WkT, "WvT": WvT,
            "bq_col": bq_col, "bk_col": bk_col, "ident": ident,
            "maskT": maskT, "gp_bc": gp_bc,
            "wown_col": wown_col,
            "wcore_col": wcore1 if half == 1 else wcore0,
            "WgT": WgT, "WuT": WuT, "WdT": WdT,
        })

    res = run_bass_kernel_spmd(nc, in_maps, core_ids=list(range(8)))

    out = np.empty((4, 4096, H), dtype=np.float32)
    for core in range(8):
        b, half = core // 2, core % 2
        s0 = half * T
        out[b, s0:s0 + T, :] = res.results[core]["out"].T
    return out
